# version 11
# baseline (speedup 1.0000x reference)
import numpy as np
import contextlib
from collections import defaultdict

import concourse.bass as bass
import concourse.mybir as mybir
from concourse.bass_utils import run_bass_kernel_spmd

F32 = mybir.dt.float32
BF16 = mybir.dt.bfloat16
AF = mybir.ActivationFunctionType
ALU = mybir.AluOpType
AX = mybir.AxisListType

D_MODEL = 1024
D_HID = 4096
NCT = 8
NHT = 32
LN_EPS = 1e-5


class Buf:
    __slots__ = ("wtok", "rtoks", "name")

    def __init__(self, name=""):
        self.wtok = None
        self.rtoks = []
        self.name = name


class Prog:
    ENGINES = ("sync", "gpsimd", "tensor", "scalar", "vector")
    DMA_ROT = {"sync": ["x0", "x1", "x2", "x3"], "gpsimd": ["w0", "w1", "w2", "w3"]}

    def __init__(self):
        self.ops = {e: [] for e in self.ENGINES}
        self.cnt = defaultdict(int)
        self.lastw = {e: defaultdict(int) for e in self.ENGINES}
        self.dma_idx = {"sync": 0, "gpsimd": 0}
        self.dma_last = {}

    def _claim(self, sem, inc):
        self.cnt[sem] += inc
        return (sem, self.cnt[sem])

    def _dedupe(self, eng, toks):
        wl = []
        for t in toks:
            s, v = t
            if self.lastw[eng][s] < v:
                wl.append(t)
                self.lastw[eng][s] = v
        return wl

    def _gather(self, reads, writes, waits):
        toks = []
        for b in reads:
            if b.wtok is not None:
                toks.append(b.wtok)
        for b in writes:
            if b.wtok is not None:
                toks.append(b.wtok)
            toks.extend(t for t in b.rtoks if t is not None)
        toks.extend(t for t in waits if t is not None)
        return toks

    def _record(self, tok, reads, writes):
        for b in reads:
            b.rtoks.append(tok)
        for b in writes:
            b.wtok = tok
            b.rtoks = []

    def emit(self, eng, fn, sem, inc=1, reads=(), writes=(), waits=()):
        wl = self._dedupe(eng, self._gather(reads, writes, waits))
        tok = self._claim(sem, inc) if sem is not None else None
        self.ops[eng].append((wl, fn, tok[0] if tok else None, inc if tok else 0))
        self._record(tok, reads, writes)
        return tok

    def mm_group(self, fns, reads=(), writes=(), waits=(), fn_reads=None):
        wl0 = self._dedupe("tensor", self._gather(reads, writes, waits))
        tok = self._claim("pe", 1)
        last = len(fns) - 1
        extra = []
        for i, fn in enumerate(fns):
            wl = list(wl0) if i == 0 else []
            if fn_reads is not None:
                fr = fn_reads[i]
                wl += self._dedupe("tensor", self._gather(fr, [], []))
                extra.extend(fr)
            self.ops["tensor"].append(
                (wl, fn, "pe" if i == last else None, 1 if i == last else 0))
        self._record(tok, list(reads) + extra, writes)
        return tok

    def emit_dma(self, queue, fn, reads=(), writes=(), waits=()):
        rot = self.DMA_ROT[queue]
        sem = rot[self.dma_idx[queue] % len(rot)]
        self.dma_idx[queue] += 1
        prev = self.dma_last.get(sem)
        w = list(waits)
        if prev is not None:
            w.append(prev)
        tok = self.emit(queue, fn, sem, 16, reads, writes, w)
        self.dma_last[sem] = tok
        return tok


def build_program(S, Tb):
    NC = min(512, Tb)
    NCH = Tb // NC
    NB = S // Tb
    LGT = Tb.bit_length() - 1
    LGS = S.bit_length() - 1
    NPOW = LGS + 2

    nc = bass.Bass("TRN2", target_bir_lowering=False, debug=False)

    xT = nc.dram_tensor("xT", [D_MODEL, S], F32, kind="ExternalInput").ap()
    w1T = nc.dram_tensor("w1T", [128, NHT * D_MODEL], F32, kind="ExternalInput").ap()
    w2T = nc.dram_tensor("w2T", [128, NHT * D_MODEL], F32, kind="ExternalInput").ap()
    h1T = nc.dram_tensor("h1T", [128, NHT * D_HID], F32, kind="ExternalInput").ap()
    h2T = nc.dram_tensor("h2T", [128, NHT * D_HID], F32, kind="ExternalInput").ap()
    w3T = nc.dram_tensor("w3T", [128, NCT * D_HID], F32, kind="ExternalInput").ap()
    wkT = nc.dram_tensor("wkT", [128, NCT * D_MODEL], F32, kind="ExternalInput").ap()
    wvT = nc.dram_tensor("wvT", [128, NCT * D_MODEL], F32, kind="ExternalInput").ap()
    wrT = nc.dram_tensor("wrT", [128, NCT * D_MODEL], F32, kind="ExternalInput").ap()
    woT = nc.dram_tensor("woT", [128, NCT * D_MODEL], F32, kind="ExternalInput").ap()
    pc = nc.dram_tensor("pc", [128, 168], F32, kind="ExternalInput").ap()
    cci = nc.dram_tensor("cci", [128, 12], F32, kind="ExternalInput").ap()
    outT = nc.dram_tensor("outT", [D_MODEL, S], F32, kind="ExternalOutput").ap()
    x1d = nc.dram_tensor("x1d", [D_MODEL, S], F32).ap()
    xnd = nc.dram_tensor("xnd", [D_MODEL, S], BF16).ap()
    x2d = nc.dram_tensor("x2d", [D_MODEL, S], F32).ap()
    ccin = nc.dram_tensor("ccin", [128, 8], F32)
    ccout = nc.dram_tensor("ccout", [8 * 128, 8], F32)

    def dview(ap):
        return ap.rearrange("(a p) t -> p a t", p=128)

    P = Prog()

    with contextlib.ExitStack() as es:
        sbuf = lambda name, shape, dt: es.enter_context(nc.sbuf_tensor(name, shape, dt))
        t_xcm = sbuf("xcm", [128, NCT * Tb], F32)
        t_xn = sbuf("xn", [128, NCT * Tb], BF16)
        H_COLS = max(NHT * Tb, NCT * S)
        G_COLS = max(NHT * Tb, 2 * NCT * 1024 + 2 * NCT * Tb)
        W_COLS = max(2 * NHT * 128, NCT * Tb, 16 * NC)
        t_h = sbuf("h", [128, H_COLS], BF16)
        t_g = sbuf("g", [128, G_COLS], BF16)
        t_wst = sbuf("wst", [128, W_COLS], BF16)
        t_tmpA = sbuf("tmpA", [128, NC], F32)
        t_tmpB = sbuf("tmpB", [128, NC], F32)
        t_tmpC = sbuf("tmpC", [128, NC], F32)
        t_scan = sbuf("scanbuf", [128, NC + 1], F32)
        t_ones = sbuf("onesNC", [128, NC], F32)
        t_ones1b = sbuf("ones1b", [128, 1], BF16)
        t_onesr = sbuf("onesr", [1, 128], F32)
        t_pc = sbuf("pcs", [128, 168], F32)
        t_ccs = sbuf("ccs", [128, 12], F32)
        t_pow = sbuf("tdpow", [128, 8 * (NPOW + 5)], F32)
        t_sg = sbuf("sg", [128, 64], F32)
        pb = [es.enter_context(nc.psum_tensor(f"pb{i}", [128, NC], F32)) for i in range(8)]

        sems = {}
        for s in ("x0", "x1", "x2", "x3", "w0", "w1", "w2", "w3", "pe", "v", "a", "cc", "p"):
            sems[s] = es.enter_context(nc.semaphore(s))

        gf32 = t_g.bitcast(F32)
        s1c = lambda oi: t_pc[:, oi:oi + 1]
        s2c = lambda oi: t_pc[:, 32 + oi:33 + oi]
        sh1c = lambda oi: t_pc[:, 64 + oi:65 + oi]
        sh2c = lambda oi: t_pc[:, 96 + oi:97 + oi]
        s3c = lambda oi: t_pc[:, 128 + oi:129 + oi]
        gcol = lambda ct: t_pc[:, 152 + ct:153 + ct]
        bcol = lambda ct: t_pc[:, 160 + ct:161 + ct]
        POW0 = 0
        C_TD = 8 * NPOW
        C_TF = C_TD + 8
        C_TDQ = C_TF + 8
        C_CAR = C_TDQ + 8
        C_SIG = C_CAR + 8
        pwc = lambda base, j: t_pow[:, base + j:base + j + 1]
        pw8 = lambda base: t_pow[:, base:base + 8]

        bXCM = [Buf(f"xcm{n}") for n in range(NCH)]
        bXN = [[Buf(f"xn{n}_{ct}") for ct in range(NCT)] for n in range(NCH)]
        bH = Buf("h")
        bG, bGD = Buf("g"), Buf("gD")
        bW0, bW1 = Buf("wst0"), Buf("wst1")
        bTA, bTB, bTC = Buf("tA"), Buf("tB"), Buf("tC")
        bSCAN, bPOW, bSG = Buf("scan"), Buf("pow"), Buf("sg")
        bCONST, bCC = Buf("const"), Buf("cc")
        bPB = [Buf(f"pb{i}") for i in range(8)]
        bX1D, bX2D, bXND = Buf("x1d"), Buf("x2d"), Buf("xnd")
        bCCIN, bCCOUT = Buf("ccin"), Buf("ccout")

        E = P.emit
        ED = P.emit_dma
        G = P.mm_group

        tk_pc = ED("sync", lambda e: e.dma_start(out=t_pc[:], in_=pc), writes=[bCONST])
        ED("sync", lambda e: e.dma_start(out=t_ccs[:], in_=cci), writes=[bCC])
        E("vector", lambda e: e.memset(t_ones[:], 1.0), "v", writes=[bCONST])
        E("vector", lambda e: e.memset(t_ones1b[:], 1.0), "v", writes=[bCONST])
        E("vector", lambda e: e.memset(t_onesr[:], 1.0), "v", writes=[bCONST])

        for base, coff in ((C_TD, 136), (C_TF, 144)):
            E("scalar", lambda e, base=base, coff=coff: e.activation(
                out=pw8(base), in_=t_pc[:, coff:coff + 8], func=AF.Exp, scale=-1.0),
              "a", reads=[bCONST], writes=[bPOW])
            E("vector", lambda e, base=base: e.tensor_scalar_add(out=pw8(base), in0=pw8(base), scalar1=1.0),
              "v", reads=[bPOW], writes=[bPOW])
            E("vector", lambda e, base=base: e.reciprocal(out=pw8(base), in_=pw8(base)),
              "v", reads=[bPOW], writes=[bPOW])
        E("vector", lambda e: e.tensor_copy(out=pw8(POW0), in_=pw8(C_TD)), "v", reads=[bPOW], writes=[bPOW])
        for k in range(1, NPOW):
            E("vector", lambda e, k=k: e.tensor_mul(
                out=pw8(POW0 + 8 * k), in0=pw8(POW0 + 8 * (k - 1)), in1=pw8(POW0 + 8 * (k - 1))),
              "v", reads=[bPOW], writes=[bPOW])
        E("vector", lambda e: e.tensor_scalar(out=pw8(C_TDQ), in0=pw8(POW0 + 8 * LGS),
                                              scalar1=t_ccs[:, 0:1], scalar2=t_ccs[:, 1:2],
                                              op0=ALU.mult, op1=ALU.add),
          "v", reads=[bPOW, bCC], writes=[bPOW])
        E("vector", lambda e: e.tensor_scalar(out=pw8(C_CAR), in0=pw8(POW0 + 8 * (LGS + 1)),
                                              scalar1=t_ccs[:, 2:3], scalar2=t_ccs[:, 3:4],
                                              op0=ALU.mult, op1=ALU.add),
          "v", reads=[bPOW, bCC], writes=[bPOW])
        E("vector", lambda e: e.tensor_mul(out=pw8(C_TDQ), in0=pw8(C_TDQ), in1=pw8(C_CAR)),
          "v", reads=[bPOW], writes=[bPOW])
        E("vector", lambda e: e.tensor_scalar_mul(out=t_pc[:, 128:136], in0=t_pc[:, 128:136], scalar1=0.5),
          "v", writes=[bCONST], waits=[tk_pc])

        xcm_tile = lambda ct: t_xcm[:, ct * Tb:(ct + 1) * Tb]
        xn_tile = lambda ct: t_xn[:, ct * Tb:(ct + 1) * Tb]

        def ln_stage(n, stage, stage_bufs):
            c0 = n * NC
            for ct in range(NCT):
                E("scalar", lambda e, ct=ct, c0=c0: e.copy(
                    out=stage(ct), in_=xcm_tile(ct)[:, c0:c0 + NC]),
                  "a", reads=[bXCM[n]], writes=list(stage_bufs))
                E("scalar", lambda e, ct=ct, c0=c0: e.activation(
                    out=stage(8 + ct), in_=xcm_tile(ct)[:, c0:c0 + NC], func=AF.Square),
                  "a", reads=[bXCM[n]], writes=list(stage_bufs))

        def ln_math(n, use_gb, fused, stage, stage_bufs):
            c0 = n * NC
            G([(lambda e, ct=ct: e.matmul(out=pb[4][0:1, :], lhsT=t_ones1b[:], rhs=stage(ct),
                                          start=(ct == 0), stop=(ct == NCT - 1)))
               for ct in range(NCT)],
              reads=list(stage_bufs) + [bCONST], writes=[bPB[4]])
            G([(lambda e, ct=ct: e.matmul(out=pb[5][0:1, :], lhsT=t_ones1b[:], rhs=stage(8 + ct),
                                          start=(ct == 0), stop=(ct == NCT - 1)))
               for ct in range(NCT)],
              reads=list(stage_bufs) + [bCONST], writes=[bPB[5]])
            mu_row = t_tmpB[0:1, :]
            rs_row = t_tmpC[0:1, :]
            fac = (1.0 + LN_EPS) if fused else 1.0
            cst = LN_EPS * LN_EPS if fused else LN_EPS
            E("vector", lambda e, mu_row=mu_row: e.tensor_scalar_mul(out=mu_row, in0=pb[4][0:1, :], scalar1=1.0 / D_MODEL),
              "v", reads=[bPB[4]], writes=[bTB])
            E("vector", lambda e, rs_row=rs_row, fac=fac, cst=cst: e.tensor_scalar(
                out=rs_row, in0=pb[5][0:1, :], scalar1=fac / D_MODEL, scalar2=cst,
                op0=ALU.mult, op1=ALU.add), "v", reads=[bPB[5]], writes=[bTC])
            E("vector", lambda e, mu_row=mu_row: e.tensor_mul(out=t_tmpA[0:1, :], in0=mu_row, in1=mu_row),
              "v", reads=[bTB], writes=[bTA])
            E("vector", lambda e, rs_row=rs_row, fac=fac: e.scalar_tensor_tensor(
                out=rs_row, in0=t_tmpA[0:1, :], scalar=-fac, in1=rs_row, op0=ALU.mult, op1=ALU.add),
              "v", reads=[bTA], writes=[bTC])
            E("scalar", lambda e, rs_row=rs_row: e.activation(out=rs_row, in_=rs_row, func=AF.Ln),
              "a", writes=[bTC])
            E("scalar", lambda e, rs_row=rs_row: e.activation(out=rs_row, in_=rs_row, func=AF.Exp, scale=-0.5),
              "a", writes=[bTC])
            G([lambda e, mu_row=mu_row: e.matmul(out=pb[6][:], lhsT=t_onesr[0:1, :], rhs=mu_row, start=True, stop=True)],
              reads=[bTB, bCONST], writes=[bPB[6]])
            G([lambda e, rs_row=rs_row: e.matmul(out=pb[7][:], lhsT=t_onesr[0:1, :], rhs=rs_row, start=True, stop=True)],
              reads=[bTC, bCONST], writes=[bPB[7]])
            for ct in range(NCT):
                E("vector", lambda e, ct=ct, c0=c0: e.tensor_sub(
                    out=t_tmpA[:], in0=xcm_tile(ct)[:, c0:c0 + NC], in1=pb[6][:]),
                  "v", reads=[bXCM[n], bPB[6]], writes=[bTA])
                if not use_gb:
                    E("vector", lambda e, ct=ct, c0=c0: e.tensor_mul(
                        out=xn_tile(ct)[:, c0:c0 + NC], in0=t_tmpA[:], in1=pb[7][:]),
                      "v", reads=[bTA, bPB[7]], writes=[bXN[n][ct]])
                else:
                    E("vector", lambda e, ct=ct, c0=c0: e.tensor_mul(
                        out=t_tmpB[:], in0=t_tmpA[:], in1=pb[7][:]),
                      "v", reads=[bTA, bPB[7]], writes=[bTB])
                    E("vector", lambda e, ct=ct, c0=c0: e.tensor_scalar(
                        out=xn_tile(ct)[:, c0:c0 + NC], in0=t_tmpB[:], scalar1=gcol(ct),
                        scalar2=bcol(ct), op0=ALU.mult, op1=ALU.add),
                      "v", reads=[bTB, bCONST], writes=[bXN[n][ct]])

        def ln_chunk(n, use_gb, fused, stage, stage_bufs):
            ln_stage(n, stage, stage_bufs)
            ln_math(n, use_gb, fused, stage, stage_bufs)

        def load_block(src_dram, src_buf, b0, n):
            return ED("sync", lambda e, b0=b0, n=n: e.dma_start(
                out=t_xcm[:].rearrange("p (a t) -> p a t", t=Tb)[:, :, n * NC:(n + 1) * NC],
                in_=dview(src_dram)[:, :, b0 + n * NC:b0 + n * NC + NC]),
                reads=[src_buf], writes=[bXCM[n]])

        def store_block(dst_dram, dst_buf, b0, n):
            return ED("sync", lambda e, b0=b0, n=n: e.dma_start(
                out=dview(dst_dram)[:, :, b0 + n * NC:b0 + n * NC + NC],
                in_=t_xcm[:].rearrange("p (a t) -> p a t", t=Tb)[:, :, n * NC:(n + 1) * NC]),
                reads=[bXCM[n]], writes=[dst_buf])

        def ffn(src_dram, src_buf, dst_dram, dst_buf):
            HW = W_COLS // 2
            wv_ah = [t_wst[:, off:off + 2 * NCT * 128].rearrange("p (w a n) -> p w a n", w=2, a=NCT)
                     for off in (0, HW)]
            wv_hh = [t_wst[:, off:off + NHT * 128].rearrange("p (a n) -> p a n", a=NHT)
                     for off in (0, HW)]
            gstage = lambda j: t_g[:, 2 * NCT * 1024 + j * NC:2 * NCT * 1024 + (j + 1) * NC]

            def w_pref_a(oi, half):
                bw = bW0 if half == 0 else bW1
                ED("gpsimd", lambda e, oi=oi, half=half: e.dma_start(
                    out=wv_ah[half][:, 0, :, :],
                    in_=w1T[:, oi * D_MODEL:(oi + 1) * D_MODEL]),
                   writes=[bw])
                ED("gpsimd", lambda e, oi=oi, half=half: e.dma_start(
                    out=wv_ah[half][:, 1, :, :],
                    in_=w2T[:, oi * D_MODEL:(oi + 1) * D_MODEL]),
                   writes=[bw])

            def w_pref_h(srcT, oi, half):
                bw = bW0 if half == 0 else bW1
                ED("gpsimd", lambda e, oi=oi, half=half, srcT=srcT: e.dma_start(
                    out=wv_hh[half][:, :, :],
                    in_=srcT[:, oi * D_HID:(oi + 1) * D_HID]),
                   writes=[bw])

            for n in range(NCH):
                load_block(src_dram, src_buf, 0, n)
            w_pref_a(0, 0)
            for n in range(NCH):
                ln_chunk(n, False, False, gstage, [bGD])

            pending_math = None
            for blk in range(NB):
                b0 = blk * Tb
                if blk > 0:
                    w_pref_a(0, 0)
                step = 0
                for n in range(NCH):
                    c0 = n * NC
                    for oi in range(NHT):
                        half = step % 2
                        bw = bW0 if half == 0 else bW1
                        if step + 1 < NCH * NHT:
                            w_pref_a((oi + 1) % NHT, 1 - half)
                        step += 1
                        if step == 11 and pending_math is not None:
                            pending_math()
                            pending_math = None
                        k1 = (oi % 2) * 2
                        k2 = k1 + 1
                        G([(lambda e, ct=ct, c0=c0, half=half, k1=k1: e.matmul(
                            out=pb[k1][:], lhsT=wv_ah[half][:, 0, ct, :],
                            rhs=t_xn[:, ct * Tb + c0:ct * Tb + c0 + NC],
                            start=(ct == 0), stop=(ct == NCT - 1))) for ct in range(NCT)],
                          reads=[bw], writes=[bPB[k1]],
                          fn_reads=[[bXN[n][ct]] for ct in range(NCT)])
                        G([(lambda e, ct=ct, c0=c0, half=half, k2=k2: e.matmul(
                            out=pb[k2][:], lhsT=wv_ah[half][:, 1, ct, :],
                            rhs=t_xn[:, ct * Tb + c0:ct * Tb + c0 + NC],
                            start=(ct == 0), stop=(ct == NCT - 1))) for ct in range(NCT)],
                          reads=[bw], writes=[bPB[k2]],
                          fn_reads=[[bXN[n][ct]] for ct in range(NCT)])
                        E("scalar", lambda e, oi=oi, k1=k1: e.activation(
                            out=t_tmpA[:], in_=pb[k1][:], func=AF.Silu, scale=s1c(oi)),
                          "a", reads=[bPB[k1], bCONST], writes=[bTA])
                        E("vector", lambda e, oi=oi, c0=c0, k2=k2: e.scalar_tensor_tensor(
                            out=t_h[:, oi * Tb + c0:oi * Tb + c0 + NC], in0=pb[k2][:],
                            scalar=s2c(oi), in1=t_tmpA[:], op0=ALU.mult, op1=ALU.mult),
                          "v", reads=[bPB[k2], bTA, bCONST], writes=[bH])

                for phase in ("h1", "h2"):
                    srcT = h1T if phase == "h1" else h2T
                    src_arena = t_h if phase == "h1" else t_g
                    dst_arena = t_g if phase == "h1" else t_h
                    src_bufs = [bH] if phase == "h1" else [bG, bGD]
                    dst_bufs = [bG, bGD] if phase == "h1" else [bH]
                    scol = sh1c if phase == "h1" else sh2c
                    w_pref_h(srcT, 0, 0)
                    for oi in range(NHT):
                        half = oi % 2
                        bw = bW0 if half == 0 else bW1
                        if oi + 1 < NHT:
                            w_pref_h(srcT, oi + 1, 1 - half)
                        bks = [(oi * NCH + n) % 4 for n in range(NCH)]
                        fns = [(lambda e, hi=hi, n=n, half=half, bk=bks[n], src_arena=src_arena: e.matmul(
                            out=pb[bk][:], lhsT=wv_hh[half][:, hi, :],
                            rhs=src_arena[:, hi * Tb + n * NC:hi * Tb + n * NC + NC],
                            start=(hi == 0), stop=(hi == NHT - 1)))
                            for hi in range(NHT) for n in range(NCH)]
                        G(fns, reads=[bw] + src_bufs, writes=[bPB[b] for b in bks])
                        for n in range(NCH):
                            c0 = n * NC
                            E("scalar", lambda e, oi=oi, c0=c0, bk=bks[n], dst_arena=dst_arena, scol=scol: e.activation(
                                out=dst_arena[:, oi * Tb + c0:oi * Tb + c0 + NC], in_=pb[bk][:],
                                func=AF.Gelu, scale=scol(oi)),
                              "a", reads=[bPB[bks[n]], bCONST], writes=dst_bufs)

                w_pref_h(w3T, 0, 0)
                pref = 0
                for n in range(NCH):
                    c0 = n * NC
                    for oi in range(NCT):
                        half = pref % 2
                        bw = bW0 if half == 0 else bW1
                        if pref + 1 < NCT * NCH:
                            w_pref_h(w3T, (pref + 1) % NCT, 1 - half)
                        pref += 1
                        if blk + 1 < NB and NCH > 1 and n == NCH - 1 and oi == 3:
                            ln_math(0, False, False, gstage, [bGD])
                        bk = (oi * NCH + n) % 4
                        G([(lambda e, hi=hi, c0=c0, half=half, bk=bk: e.matmul(
                            out=pb[bk][:], lhsT=wv_hh[half][:, hi, :],
                            rhs=t_h[:, hi * Tb + c0:hi * Tb + c0 + NC],
                            start=(hi == 0), stop=(hi == NHT - 1))) for hi in range(NHT)],
                          reads=[bw, bH], writes=[bPB[bk]])
                        E("vector", lambda e, oi=oi, c0=c0, bk=bk: e.scalar_tensor_tensor(
                            out=gf32[:, oi * Tb + c0:oi * Tb + c0 + NC], in0=pb[bk][:],
                            scalar=s3c(oi), in1=t_xcm[:, oi * Tb + c0:oi * Tb + c0 + NC],
                            op0=ALU.mult, op1=ALU.add),
                          "v", reads=[bPB[bk], bXCM[n], bCONST], writes=[bG])
                    ED("sync", lambda e, b0=b0, n=n: e.dma_start(
                        out=dview(dst_dram)[:, :, b0 + n * NC:b0 + n * NC + NC],
                        in_=gf32[:, 0:NCT * Tb].rearrange("p (a t) -> p a t", t=Tb)[:, :, n * NC:(n + 1) * NC]),
                       reads=[bG], writes=[dst_buf])
                    if blk + 1 < NB:
                        load_block(src_dram, src_buf, b0 + Tb, n)
                        ln_stage(n, gstage, [bGD])
                if blk + 1 < NB:
                    if NCH > 1:
                        pending_math = (lambda: ln_math(1, False, False, gstage, [bGD]))
                    else:
                        ln_math(0, False, False, gstage, [bGD])

        awv = t_g[:, 0:2 * NCT * 1024].rearrange("p (w a n) -> p w a n", w=2, a=NCT)
        DRES_F32 = NCT * 1024
        dres = lambda dt: gf32[:, DRES_F32 + dt * Tb:DRES_F32 + (dt + 1) * Tb]
        wstage = lambda j: t_wst[:, j * NC:(j + 1) * NC]

        def build_dres():
            for dt in range(NCT):
                E("vector", lambda e, dt=dt: e.tensor_copy(out=dres(dt)[:, 0:1], in_=pwc(C_TDQ, dt)),
                  "v", reads=[bPOW], writes=[bGD])
                for k in range(LGT):
                    E("vector", lambda e, dt=dt, k=k: e.tensor_scalar_mul(
                        out=dres(dt)[:, 1 << k:2 << k], in0=dres(dt)[:, 0:1 << k],
                        scalar1=pwc(POW0 + 8 * k, dt)), "v", reads=[bPOW, bGD], writes=[bGD])

        def attn():
            ED("gpsimd", lambda e: e.dma_start(out=awv[:, 0, :, :], in_=wkT), writes=[bG])
            ED("gpsimd", lambda e: e.dma_start(out=awv[:, 1, :, :], in_=wvT), writes=[bG])
            build_dres()
            E("vector", lambda e: e.memset(pw8(C_SIG), 0.0), "v", writes=[bPOW])

            def store_xn(b0):
                for n in range(NCH):
                    ED("sync", lambda e, b0=b0, n=n: e.dma_start(
                        out=dview(xnd)[:, :, b0 + n * NC:b0 + n * NC + NC],
                        in_=t_xn[:].rearrange("p (a t) -> p a t", t=Tb)[:, :, n * NC:(n + 1) * NC]),
                       reads=[bXN[n][ct] for ct in range(NCT)], writes=[bXND])

            for n in range(NCH):
                load_block(x1d, bX1D, 0, n)
            for n in range(NCH):
                ln_chunk(n, True, True, wstage, [bW0, bW1])
            store_xn(0)
            for blk in range(NB):
                b0 = blk * Tb
                for dt in range(NCT):
                    if blk + 1 < NB and dt == 1:
                        for n in range(NCH):
                            load_block(x1d, bX1D, b0 + Tb, n)
                        ln_stage(0, wstage, [bW0, bW1])
                    for n in range(NCH):
                        c0 = n * NC
                        idx = (dt * NCH + n) % 2
                        kk, kv_ = 2 * idx, 2 * idx + 1
                        G([(lambda e, ct=ct, dt=dt, c0=c0, kk=kk: e.matmul(
                            out=pb[kk][:], lhsT=awv[:, 0, ct, dt * 128:(dt + 1) * 128],
                            rhs=t_xn[:, ct * Tb + c0:ct * Tb + c0 + NC],
                            start=(ct == 0), stop=(ct == NCT - 1))) for ct in range(NCT)],
                          reads=[bG], writes=[bPB[kk]],
                          fn_reads=[[bXN[n][ct]] for ct in range(NCT)])
                        G([(lambda e, ct=ct, dt=dt, c0=c0, kv_=kv_: e.matmul(
                            out=pb[kv_][:], lhsT=awv[:, 1, ct, dt * 128:(dt + 1) * 128],
                            rhs=t_xn[:, ct * Tb + c0:ct * Tb + c0 + NC],
                            start=(ct == 0), stop=(ct == NCT - 1))) for ct in range(NCT)],
                          reads=[bG], writes=[bPB[kv_]],
                          fn_reads=[[bXN[n][ct]] for ct in range(NCT)])
                        E("scalar", lambda e, kk=kk: e.copy(out=t_tmpA[:], in_=pb[kk][:]),
                          "a", reads=[bPB[kk]], writes=[bTA])
                        E("vector", lambda e, dt=dt, b0=b0, c0=c0, kv_=kv_: e.tensor_mul(
                            out=t_h[:, dt * S + b0 + c0:dt * S + b0 + c0 + NC],
                            in0=t_tmpA[:], in1=pb[kv_][:]),
                          "v", reads=[bTA, bPB[kv_]], writes=[bH])
                        E("vector", lambda e, dt=dt, b0=b0, c0=c0: e.scalar_tensor_tensor(
                            out=t_tmpB[:], in0=t_h[:, dt * S + b0 + c0:dt * S + b0 + c0 + NC],
                            scalar=1.0, in1=dres(dt)[:, c0:c0 + NC], op0=ALU.mult, op1=ALU.mult,
                            accum_out=t_tmpC[:, 0:1]), "v", reads=[bH, bGD], writes=[bTB, bTC])
                        E("vector", lambda e, dt=dt: e.tensor_add(
                            out=pwc(C_SIG, dt), in0=pwc(C_SIG, dt), in1=t_tmpC[:, 0:1]),
                          "v", reads=[bTC], writes=[bPOW])
                if blk + 1 < NB:
                    ln_math(0, True, True, wstage, [bW0, bW1])
                    for n in range(1, NCH):
                        ln_stage(n, wstage, [bW0, bW1])
                        ln_math(n, True, True, wstage, [bW0, bW1])
                    store_xn(b0 + Tb)
                for dt in range(NCT):
                    E("vector", lambda e, dt=dt: e.tensor_scalar_mul(
                        out=dres(dt)[:], in0=dres(dt)[:], scalar1=pwc(POW0 + 8 * LGT, dt)),
                      "v", reads=[bPOW], writes=[bGD])

            ED("gpsimd", lambda e: e.dma_start(out=ccin.ap(), in_=pw8(C_SIG)),
               reads=[bPOW], writes=[bCCIN])
            ED("gpsimd", lambda e: e.dma_start(out=awv[:, 0, :, :], in_=wrT), writes=[bG])
            ED("gpsimd", lambda e: e.dma_start(out=awv[:, 1, :, :], in_=woT), writes=[bG])
            E("gpsimd", lambda e: e.collective_compute(
                "AllGather", ALU.bypass, replica_groups=[list(range(8))],
                ins=[ccin.ap().opt()], outs=[ccout.ap().opt()]), "cc", 1,
              reads=[bCCIN], writes=[bCCOUT])
            ED("gpsimd", lambda e: e.dma_start(
                out=t_sg[:].rearrange("p (r c) -> p r c", c=8),
                in_=ccout.ap().rearrange("(r p) c -> p r c", p=128)),
               reads=[bCCOUT], writes=[bSG])

            def load_xn(b0):
                for n in range(NCH):
                    ED("sync", lambda e, b0=b0, n=n: e.dma_start(
                        out=t_xn[:].rearrange("p (a t) -> p a t", t=Tb)[:, :, n * NC:(n + 1) * NC],
                        in_=dview(xnd)[:, :, b0 + n * NC:b0 + n * NC + NC]),
                       reads=[bXND], writes=[bXN[n][ct] for ct in range(NCT)])

            build_dres()
            for n in range(NCH):
                load_block(x1d, bX1D, 0, n)
            load_xn(0)
            E("vector", lambda e: e.memset(pw8(C_CAR), 0.0), "v", writes=[bPOW])
            sgv = t_sg[:].rearrange("p (r c) -> p r c", c=8)
            for r in range(8):
                E("vector", lambda e, r=r: e.scalar_tensor_tensor(
                    out=pw8(C_CAR), in0=sgv[:, r, :], scalar=t_ccs[:, 4 + r:5 + r], in1=pw8(C_CAR),
                    op0=ALU.mult, op1=ALU.add), "v", reads=[bSG, bCC, bPOW], writes=[bPOW])

            for blk in range(NB):
                b0 = blk * Tb
                for dt in range(NCT):
                    for n in range(NCH):
                        c0 = n * NC
                        bk = (dt * NCH + n) % 4
                        G([(lambda e, ct=ct, dt=dt, c0=c0, bk=bk: e.matmul(
                            out=pb[bk][:], lhsT=awv[:, 0, ct, dt * 128:(dt + 1) * 128],
                            rhs=t_xn[:, ct * Tb + c0:ct * Tb + c0 + NC],
                            start=(ct == 0), stop=(ct == NCT - 1))) for ct in range(NCT)],
                          reads=[bG], writes=[bPB[bk]],
                          fn_reads=[[bXN[n][ct]] for ct in range(NCT)])
                        E("scalar", lambda e, bk=bk: e.activation(
                            out=t_tmpA[:], in_=pb[bk][:], func=AF.Sigmoid),
                          "a", reads=[bPB[bk]], writes=[bTA])
                        E("vector", lambda e, dt=dt, b0=b0, c0=c0: e.tensor_mul(
                            out=t_tmpB[:], in0=t_h[:, dt * S + b0 + c0:dt * S + b0 + c0 + NC],
                            in1=dres(dt)[:, c0:c0 + NC]), "v", reads=[bH, bGD], writes=[bTB])
                        E("vector", lambda e, dt=dt: e.tensor_copy(out=t_scan[:, 0:1], in_=pwc(C_CAR, dt)),
                          "v", reads=[bPOW], writes=[bSCAN])
                        E("vector", lambda e, dt=dt: e.tensor_tensor_scan(
                            out=t_scan[:, 1:NC + 1], data0=t_ones[:], data1=t_tmpB[:],
                            initial=pwc(C_CAR, dt), op0=ALU.mult, op1=ALU.add),
                          "v", reads=[bTB, bPOW, bCONST], writes=[bSCAN])
                        E("vector", lambda e, dt=dt: e.tensor_copy(out=pwc(C_CAR, dt), in_=t_scan[:, NC:NC + 1]),
                          "v", reads=[bSCAN], writes=[bPOW])
                        E("scalar", lambda e, dt=dt, b0=b0, c0=c0: e.mul(
                            out=t_tmpC[:], in_=t_h[:, dt * S + b0 + c0:dt * S + b0 + c0 + NC],
                            mul=pwc(C_TF, dt)), "a", reads=[bH, bPOW], writes=[bTC])
                        E("vector", lambda e, dt=dt: e.scalar_tensor_tensor(
                            out=t_tmpB[:], in0=t_scan[:, 0:NC], scalar=pwc(C_TD, dt), in1=t_tmpC[:],
                            op0=ALU.mult, op1=ALU.add), "v", reads=[bSCAN, bTC, bPOW], writes=[bTB])
                        E("gpsimd", lambda e, dt=dt, c0=c0: e.tensor_mul(
                            out=t_wst[:, dt * Tb + c0:dt * Tb + c0 + NC], in0=t_tmpB[:], in1=t_tmpA[:]),
                          "p", reads=[bTB, bTA], writes=[bW0 if dt < NCT // 2 else bW1])
                for ct in range(NCT):
                    for n in range(NCH):
                        c0 = n * NC
                        bk = (ct * NCH + n) % 4
                        G([(lambda e, dt=dt, ct=ct, c0=c0, bk=bk: e.matmul(
                            out=pb[bk][:], lhsT=awv[:, 1, dt, ct * 128:(ct + 1) * 128],
                            rhs=t_wst[:, dt * Tb + c0:dt * Tb + c0 + NC],
                            start=(dt == 0), stop=(dt == NCT - 1))) for dt in range(NCT)],
                          reads=[bG, bW0, bW1], writes=[bPB[bk]])
                        E("vector", lambda e, ct=ct, c0=c0, bk=bk: e.tensor_add(
                            out=t_xcm[:, ct * Tb + c0:ct * Tb + c0 + NC], in0=pb[bk][:],
                            in1=t_xcm[:, ct * Tb + c0:ct * Tb + c0 + NC]),
                          "v", reads=[bPB[bk]], writes=[bXCM[n]])
                for n in range(NCH):
                    store_block(x2d, bX2D, b0, n)
                    if blk + 1 < NB:
                        load_block(x1d, bX1D, b0 + Tb, n)
                if blk + 1 < NB:
                    load_xn(b0 + Tb)
                for dt in range(NCT):
                    E("vector", lambda e, dt=dt: e.tensor_scalar_mul(
                        out=dres(dt)[:], in0=dres(dt)[:], scalar1=pwc(POW0 + 8 * LGT, dt)),
                      "v", reads=[bPOW], writes=[bGD])

        ffn(xT, Buf("xTsrc"), x1d, bX1D)
        attn()
        ffn(x2d, bX2D, outT, Buf("outDst"))

        final_waits = [P.dma_last[s] for s in P.DMA_ROT["sync"] if s in P.dma_last]
        final_waits += [P.dma_last[s] for s in P.DMA_ROT["gpsimd"] if s in P.dma_last]
        P.ops["sync"].append((P._dedupe("sync", final_waits), None, None, 0))

        block = es.enter_context(nc.Block())

        def replay(eng_obj, name):
            for wl, fn, sem, inc in P.ops[name]:
                for (s, v) in wl:
                    eng_obj.wait_ge(sems[s], v)
                if fn is not None:
                    ins = fn(eng_obj)
                    if sem is not None:
                        ins.then_inc(sems[sem], inc)

        @block.sync
        def _(e):
            replay(e, "sync")

        @block.gpsimd
        def _(e):
            replay(e, "gpsimd")

        @block.tensor
        def _(e):
            replay(e, "tensor")

        @block.scalar
        def _(e):
            replay(e, "scalar")

        @block.vector
        def _(e):
            replay(e, "vector")

    return nc


def _prep_inputs(inputs, S):
    x = np.asarray(inputs["L0_attn_x"])
    B, T, C = x.shape
    q_per_b = T // S
    n_cores = B * q_per_b
    col = lambda v: np.ascontiguousarray(np.asarray(v, dtype=np.float32).reshape(-1, 128).T)

    def tile4(a, no, nj, na):
        a = np.asarray(a, dtype=np.float32).reshape(no, nj, na, 128)
        return np.ascontiguousarray(a.transpose(3, 0, 2, 1).reshape(128, no * na * nj))

    shared = dict(
        w1T=tile4(inputs["L0_ffn_w1_w"], 32, 128, 8),
        w2T=tile4(inputs["L0_ffn_w2_w"], 32, 128, 8),
        h1T=tile4(inputs["L0_ffn_h1_w"], 32, 128, 32),
        h2T=tile4(inputs["L0_ffn_h2_w"], 32, 128, 32),
        w3T=tile4(inputs["L0_ffn_w3_w"], 8, 128, 32),
        wkT=tile4(inputs["L0_attn_Wk"], 1, 1024, 8),
        wvT=tile4(inputs["L0_attn_Wv"], 1, 1024, 8),
        wrT=tile4(inputs["L0_attn_Wr"], 1, 1024, 8),
        woT=tile4(inputs["L0_attn_Wo"], 1, 1024, 8),
    )
    pcarr = np.concatenate([
        col(inputs["L0_ffn_w1_s"]), col(inputs["L0_ffn_w2_s"]),
        col(inputs["L0_ffn_h1_s"]), col(inputs["L0_ffn_h2_s"]),
        col(inputs["L0_ffn_w3_s"]),
        col(inputs["L0_attn_time_decay"]), col(inputs["L0_attn_time_first"]),
        col(inputs["L0_attn_ln_gamma"]), col(inputs["L0_attn_ln_beta"]),
    ], axis=1).astype(np.float32)
    in_maps = []
    for c in range(n_cores):
        b, q = divmod(c, q_per_b)
        xs = np.ascontiguousarray(x[b, q * S:(q + 1) * S, :].astype(np.float32).T)
        qb1, qb2 = float(q & 1), float((q >> 1) & 1)
        ccarr = np.zeros((128, 12), np.float32)
        ccarr[:, 0] = qb1
        ccarr[:, 1] = 1.0 - qb1
        ccarr[:, 2] = qb2
        ccarr[:, 3] = 1.0 - qb2
        for r in range(n_cores):
            rb, rq = divmod(r, q_per_b)
            if rb == b and rq < q:
                ccarr[:, 4 + r] = 1.0
        m = dict(shared)
        m["xT"] = xs
        m["pc"] = pcarr
        m["cci"] = ccarr
        in_maps.append(m)
    return in_maps, n_cores


def kernel(**inputs):
    x = np.asarray(inputs["L0_attn_x"])
    B, T, C = x.shape
    S = (B * T) // 8
    Tb = min(1024, S)
    in_maps, n_cores = _prep_inputs(inputs, S)
    nc = build_program(S, Tb)
    res = run_bass_kernel_spmd(nc, in_maps, core_ids=list(range(n_cores)))
    out = np.empty((B, T, C), np.float32)
    q_per_b = T // S
    for c in range(n_cores):
        b, q = divmod(c, q_per_b)
        out[b, q * S:(q + 1) * S, :] = np.asarray(res.results[c]["outT"]).T
    return out


# version 12
# speedup vs baseline: 1.1393x; 1.1393x over previous
import numpy as np
import contextlib
from collections import defaultdict

import concourse.bass as bass
import concourse.mybir as mybir
from concourse.bass_utils import run_bass_kernel_spmd

F32 = mybir.dt.float32
BF16 = mybir.dt.bfloat16
AF = mybir.ActivationFunctionType
ALU = mybir.AluOpType
AX = mybir.AxisListType

D_MODEL = 1024
D_HID = 4096
NCT = 8
NHT = 32
LN_EPS = 1e-5


class Buf:
    __slots__ = ("wtok", "rtoks", "name")

    def __init__(self, name=""):
        self.wtok = None
        self.rtoks = []
        self.name = name


class Prog:
    ENGINES = ("sync", "gpsimd", "tensor", "scalar", "vector")
    DMA_ROT = {"sync": ["x0", "x1", "x2", "x3"], "gpsimd": ["w0", "w1", "w2", "w3"]}

    def __init__(self):
        self.ops = {e: [] for e in self.ENGINES}
        self.cnt = defaultdict(int)
        self.lastw = {e: defaultdict(int) for e in self.ENGINES}
        self.dma_idx = {"sync": 0, "gpsimd": 0}
        self.dma_last = {}

    def _claim(self, sem, inc):
        self.cnt[sem] += inc
        return (sem, self.cnt[sem])

    def _dedupe(self, eng, toks):
        wl = []
        for t in toks:
            s, v = t
            if self.lastw[eng][s] < v:
                wl.append(t)
                self.lastw[eng][s] = v
        return wl

    def _gather(self, reads, writes, waits):
        toks = []
        for b in reads:
            if b.wtok is not None:
                toks.append(b.wtok)
        for b in writes:
            if b.wtok is not None:
                toks.append(b.wtok)
            toks.extend(t for t in b.rtoks if t is not None)
        toks.extend(t for t in waits if t is not None)
        return toks

    def _record(self, tok, reads, writes):
        for b in reads:
            b.rtoks.append(tok)
        for b in writes:
            b.wtok = tok
            b.rtoks = []

    def emit(self, eng, fn, sem, inc=1, reads=(), writes=(), waits=()):
        wl = self._dedupe(eng, self._gather(reads, writes, waits))
        tok = self._claim(sem, inc) if sem is not None else None
        self.ops[eng].append((wl, fn, tok[0] if tok else None, inc if tok else 0))
        self._record(tok, reads, writes)
        return tok

    def mm_group(self, fns, reads=(), writes=(), waits=(), fn_reads=None):
        wl0 = self._dedupe("tensor", self._gather(reads, writes, waits))
        tok = self._claim("pe", 1)
        last = len(fns) - 1
        extra = []
        for i, fn in enumerate(fns):
            wl = list(wl0) if i == 0 else []
            if fn_reads is not None:
                fr = fn_reads[i]
                wl += self._dedupe("tensor", self._gather(fr, [], []))
                extra.extend(fr)
            self.ops["tensor"].append(
                (wl, fn, "pe" if i == last else None, 1 if i == last else 0))
        self._record(tok, list(reads) + extra, writes)
        return tok

    def emit_dma(self, queue, fn, reads=(), writes=(), waits=()):
        rot = self.DMA_ROT[queue]
        sem = rot[self.dma_idx[queue] % len(rot)]
        self.dma_idx[queue] += 1
        prev = self.dma_last.get(sem)
        w = list(waits)
        if prev is not None:
            w.append(prev)
        tok = self.emit(queue, fn, sem, 16, reads, writes, w)
        self.dma_last[sem] = tok
        return tok


def build_program(S, Tb):
    NC = min(512, Tb)
    NCH = Tb // NC
    NB = S // Tb
    LGT = Tb.bit_length() - 1
    LGS = S.bit_length() - 1
    NPOW = LGS + 2

    nc = bass.Bass("TRN2", target_bir_lowering=False, debug=False)

    xT = nc.dram_tensor("xT", [D_MODEL, S], F32, kind="ExternalInput").ap()
    w1T = nc.dram_tensor("w1T", [128, NHT * D_MODEL], F32, kind="ExternalInput").ap()
    w2T = nc.dram_tensor("w2T", [128, NHT * D_MODEL], F32, kind="ExternalInput").ap()
    h1T = nc.dram_tensor("h1T", [128, NHT * D_HID], F32, kind="ExternalInput").ap()
    h2T = nc.dram_tensor("h2T", [128, NHT * D_HID], F32, kind="ExternalInput").ap()
    w3T = nc.dram_tensor("w3T", [128, NCT * D_HID], F32, kind="ExternalInput").ap()
    wkT = nc.dram_tensor("wkT", [128, NCT * D_MODEL], F32, kind="ExternalInput").ap()
    wvT = nc.dram_tensor("wvT", [128, NCT * D_MODEL], F32, kind="ExternalInput").ap()
    wrT = nc.dram_tensor("wrT", [128, NCT * D_MODEL], F32, kind="ExternalInput").ap()
    woT = nc.dram_tensor("woT", [128, NCT * D_MODEL], F32, kind="ExternalInput").ap()
    pc = nc.dram_tensor("pc", [128, 168], F32, kind="ExternalInput").ap()
    cci = nc.dram_tensor("cci", [128, 12], F32, kind="ExternalInput").ap()
    outT = nc.dram_tensor("outT", [D_MODEL, S], F32, kind="ExternalOutput").ap()
    x1d = nc.dram_tensor("x1d", [D_MODEL, S], F32).ap()
    xnd = nc.dram_tensor("xnd", [D_MODEL, S], BF16).ap()
    x2d = nc.dram_tensor("x2d", [D_MODEL, S], F32).ap()
    ccin = nc.dram_tensor("ccin", [128, 8], F32)
    ccout = nc.dram_tensor("ccout", [8 * 128, 8], F32)

    def dview(ap):
        return ap.rearrange("(a p) t -> p a t", p=128)

    P = Prog()

    with contextlib.ExitStack() as es:
        sbuf = lambda name, shape, dt: es.enter_context(nc.sbuf_tensor(name, shape, dt))
        t_xcm = sbuf("xcm", [128, NCT * Tb], F32)
        t_xn = sbuf("xn", [128, NCT * Tb], BF16)
        H_COLS = max(NHT * Tb, NCT * S)
        G_COLS = max(NHT * Tb, 2 * NCT * 1024 + 2 * NCT * Tb)
        W_COLS = max(2 * NHT * 128, NCT * Tb, 16 * NC)
        t_h = sbuf("h", [128, H_COLS], BF16)
        t_g = sbuf("g", [128, G_COLS], BF16)
        t_wst = sbuf("wst", [128, W_COLS], BF16)
        t_tmpA = sbuf("tmpA", [128, NC], F32)
        t_tmpB = sbuf("tmpB", [128, NC], F32)
        t_tmpC = sbuf("tmpC", [128, NC], F32)
        t_scan = sbuf("scanbuf", [128, NC + 1], F32)
        t_ones = sbuf("onesNC", [128, NC], F32)
        t_ones1b = sbuf("ones1b", [128, 1], BF16)
        t_onesr = sbuf("onesr", [1, 128], F32)
        t_pc = sbuf("pcs", [128, 168], F32)
        t_ccs = sbuf("ccs", [128, 12], F32)
        t_pow = sbuf("tdpow", [128, 8 * (NPOW + 5)], F32)
        t_sg = sbuf("sg", [128, 64], F32)
        pb = [es.enter_context(nc.psum_tensor(f"pb{i}", [128, NC], F32)) for i in range(8)]

        sems = {}
        for s in ("x0", "x1", "x2", "x3", "w0", "w1", "w2", "w3", "pe", "v", "a", "cc", "p"):
            sems[s] = es.enter_context(nc.semaphore(s))

        gf32 = t_g.bitcast(F32)
        s1c = lambda oi: t_pc[:, oi:oi + 1]
        s2c = lambda oi: t_pc[:, 32 + oi:33 + oi]
        sh1c = lambda oi: t_pc[:, 64 + oi:65 + oi]
        sh2c = lambda oi: t_pc[:, 96 + oi:97 + oi]
        s3c = lambda oi: t_pc[:, 128 + oi:129 + oi]
        gcol = lambda ct: t_pc[:, 152 + ct:153 + ct]
        bcol = lambda ct: t_pc[:, 160 + ct:161 + ct]
        POW0 = 0
        C_TD = 8 * NPOW
        C_TF = C_TD + 8
        C_TDQ = C_TF + 8
        C_CAR = C_TDQ + 8
        C_SIG = C_CAR + 8
        pwc = lambda base, j: t_pow[:, base + j:base + j + 1]
        pw8 = lambda base: t_pow[:, base:base + 8]

        bXCM = [Buf(f"xcm{n}") for n in range(NCH)]
        bXN = [[Buf(f"xn{n}_{ct}") for ct in range(NCT)] for n in range(NCH)]
        bH = Buf("h")
        bG, bGD = Buf("g"), Buf("gD")
        bW0, bW1 = Buf("wst0"), Buf("wst1")
        bTA, bTB, bTC = Buf("tA"), Buf("tB"), Buf("tC")
        bSCAN, bPOW, bSG = Buf("scan"), Buf("pow"), Buf("sg")
        bCONST, bCC = Buf("const"), Buf("cc")
        bPB = [Buf(f"pb{i}") for i in range(8)]
        bX1D, bX2D, bXND = Buf("x1d"), Buf("x2d"), Buf("xnd")
        bCCIN, bCCOUT = Buf("ccin"), Buf("ccout")

        E = P.emit
        ED = P.emit_dma
        G = P.mm_group

        tk_pc = ED("sync", lambda e: e.dma_start(out=t_pc[:], in_=pc), writes=[bCONST])
        ED("sync", lambda e: e.dma_start(out=t_ccs[:], in_=cci), writes=[bCC])
        E("vector", lambda e: e.memset(t_ones[:], 1.0), "v", writes=[bCONST])
        E("vector", lambda e: e.memset(t_ones1b[:], 1.0), "v", writes=[bCONST])
        E("vector", lambda e: e.memset(t_onesr[:], 1.0), "v", writes=[bCONST])

        for base, coff in ((C_TD, 136), (C_TF, 144)):
            E("scalar", lambda e, base=base, coff=coff: e.activation(
                out=pw8(base), in_=t_pc[:, coff:coff + 8], func=AF.Exp, scale=-1.0),
              "a", reads=[bCONST], writes=[bPOW])
            E("vector", lambda e, base=base: e.tensor_scalar_add(out=pw8(base), in0=pw8(base), scalar1=1.0),
              "v", reads=[bPOW], writes=[bPOW])
            E("vector", lambda e, base=base: e.reciprocal(out=pw8(base), in_=pw8(base)),
              "v", reads=[bPOW], writes=[bPOW])
        E("vector", lambda e: e.tensor_copy(out=pw8(POW0), in_=pw8(C_TD)), "v", reads=[bPOW], writes=[bPOW])
        for k in range(1, NPOW):
            E("vector", lambda e, k=k: e.tensor_mul(
                out=pw8(POW0 + 8 * k), in0=pw8(POW0 + 8 * (k - 1)), in1=pw8(POW0 + 8 * (k - 1))),
              "v", reads=[bPOW], writes=[bPOW])
        E("vector", lambda e: e.tensor_scalar(out=pw8(C_TDQ), in0=pw8(POW0 + 8 * LGS),
                                              scalar1=t_ccs[:, 0:1], scalar2=t_ccs[:, 1:2],
                                              op0=ALU.mult, op1=ALU.add),
          "v", reads=[bPOW, bCC], writes=[bPOW])
        E("vector", lambda e: e.tensor_scalar(out=pw8(C_CAR), in0=pw8(POW0 + 8 * (LGS + 1)),
                                              scalar1=t_ccs[:, 2:3], scalar2=t_ccs[:, 3:4],
                                              op0=ALU.mult, op1=ALU.add),
          "v", reads=[bPOW, bCC], writes=[bPOW])
        E("vector", lambda e: e.tensor_mul(out=pw8(C_TDQ), in0=pw8(C_TDQ), in1=pw8(C_CAR)),
          "v", reads=[bPOW], writes=[bPOW])
        E("vector", lambda e: e.tensor_scalar_mul(out=t_pc[:, 128:136], in0=t_pc[:, 128:136], scalar1=0.5),
          "v", writes=[bCONST], waits=[tk_pc])

        xcm_tile = lambda ct: t_xcm[:, ct * Tb:(ct + 1) * Tb]
        xn_tile = lambda ct: t_xn[:, ct * Tb:(ct + 1) * Tb]

        def ln_stage(n, stage, stage_bufs):
            c0 = n * NC
            for ct in range(NCT):
                E("scalar", lambda e, ct=ct, c0=c0: e.copy(
                    out=stage(ct), in_=xcm_tile(ct)[:, c0:c0 + NC]),
                  "a", reads=[bXCM[n]], writes=list(stage_bufs))
                E("scalar", lambda e, ct=ct, c0=c0: e.activation(
                    out=stage(8 + ct), in_=xcm_tile(ct)[:, c0:c0 + NC], func=AF.Square),
                  "a", reads=[bXCM[n]], writes=list(stage_bufs))

        def ln_math(n, use_gb, fused, stage, stage_bufs):
            c0 = n * NC
            G([(lambda e, ct=ct: e.matmul(out=pb[4][0:1, :], lhsT=t_ones1b[:], rhs=stage(ct),
                                          start=(ct == 0), stop=(ct == NCT - 1)))
               for ct in range(NCT)],
              reads=list(stage_bufs) + [bCONST], writes=[bPB[4]])
            G([(lambda e, ct=ct: e.matmul(out=pb[5][0:1, :], lhsT=t_ones1b[:], rhs=stage(8 + ct),
                                          start=(ct == 0), stop=(ct == NCT - 1)))
               for ct in range(NCT)],
              reads=list(stage_bufs) + [bCONST], writes=[bPB[5]])
            mu_row = t_tmpB[0:1, :]
            rs_row = t_tmpC[0:1, :]
            fac = (1.0 + LN_EPS) if fused else 1.0
            cst = LN_EPS * LN_EPS if fused else LN_EPS
            E("vector", lambda e, mu_row=mu_row: e.tensor_scalar_mul(out=mu_row, in0=pb[4][0:1, :], scalar1=1.0 / D_MODEL),
              "v", reads=[bPB[4]], writes=[bTB])
            E("vector", lambda e, rs_row=rs_row, fac=fac, cst=cst: e.tensor_scalar(
                out=rs_row, in0=pb[5][0:1, :], scalar1=fac / D_MODEL, scalar2=cst,
                op0=ALU.mult, op1=ALU.add), "v", reads=[bPB[5]], writes=[bTC])
            E("vector", lambda e, mu_row=mu_row: e.tensor_mul(out=t_tmpA[0:1, :], in0=mu_row, in1=mu_row),
              "v", reads=[bTB], writes=[bTA])
            E("vector", lambda e, rs_row=rs_row, fac=fac: e.scalar_tensor_tensor(
                out=rs_row, in0=t_tmpA[0:1, :], scalar=-fac, in1=rs_row, op0=ALU.mult, op1=ALU.add),
              "v", reads=[bTA], writes=[bTC])
            E("scalar", lambda e, rs_row=rs_row: e.activation(out=rs_row, in_=rs_row, func=AF.Ln),
              "a", writes=[bTC])
            E("scalar", lambda e, rs_row=rs_row: e.activation(out=rs_row, in_=rs_row, func=AF.Exp, scale=-0.5),
              "a", writes=[bTC])
            G([lambda e, mu_row=mu_row: e.matmul(out=pb[6][:], lhsT=t_onesr[0:1, :], rhs=mu_row, start=True, stop=True)],
              reads=[bTB, bCONST], writes=[bPB[6]])
            G([lambda e, rs_row=rs_row: e.matmul(out=pb[7][:], lhsT=t_onesr[0:1, :], rhs=rs_row, start=True, stop=True)],
              reads=[bTC, bCONST], writes=[bPB[7]])
            for ct in range(NCT):
                E("vector", lambda e, ct=ct, c0=c0: e.tensor_sub(
                    out=t_tmpA[:], in0=xcm_tile(ct)[:, c0:c0 + NC], in1=pb[6][:]),
                  "v", reads=[bXCM[n], bPB[6]], writes=[bTA])
                if not use_gb:
                    E("vector", lambda e, ct=ct, c0=c0: e.tensor_mul(
                        out=xn_tile(ct)[:, c0:c0 + NC], in0=t_tmpA[:], in1=pb[7][:]),
                      "v", reads=[bTA, bPB[7]], writes=[bXN[n][ct]])
                else:
                    E("vector", lambda e, ct=ct, c0=c0: e.tensor_mul(
                        out=t_tmpB[:], in0=t_tmpA[:], in1=pb[7][:]),
                      "v", reads=[bTA, bPB[7]], writes=[bTB])
                    E("vector", lambda e, ct=ct, c0=c0: e.tensor_scalar(
                        out=xn_tile(ct)[:, c0:c0 + NC], in0=t_tmpB[:], scalar1=gcol(ct),
                        scalar2=bcol(ct), op0=ALU.mult, op1=ALU.add),
                      "v", reads=[bTB, bCONST], writes=[bXN[n][ct]])

        def ln_chunk(n, use_gb, fused, stage, stage_bufs):
            ln_stage(n, stage, stage_bufs)
            ln_math(n, use_gb, fused, stage, stage_bufs)

        def load_block(src_dram, src_buf, b0, n):
            return ED("sync", lambda e, b0=b0, n=n: e.dma_start(
                out=t_xcm[:].rearrange("p (a t) -> p a t", t=Tb)[:, :, n * NC:(n + 1) * NC],
                in_=dview(src_dram)[:, :, b0 + n * NC:b0 + n * NC + NC]),
                reads=[src_buf], writes=[bXCM[n]])

        def store_block(dst_dram, dst_buf, b0, n):
            return ED("sync", lambda e, b0=b0, n=n: e.dma_start(
                out=dview(dst_dram)[:, :, b0 + n * NC:b0 + n * NC + NC],
                in_=t_xcm[:].rearrange("p (a t) -> p a t", t=Tb)[:, :, n * NC:(n + 1) * NC]),
                reads=[bXCM[n]], writes=[dst_buf])

        def ffn(src_dram, src_buf, dst_dram, dst_buf):
            HW = W_COLS // 2
            wv_ah = [t_wst[:, off:off + 2 * NCT * 128].rearrange("p (w a n) -> p w a n", w=2, a=NCT)
                     for off in (0, HW)]
            wv_hh = [t_wst[:, off:off + NHT * 128].rearrange("p (a n) -> p a n", a=NHT)
                     for off in (0, HW)]
            gstage = lambda j: t_g[:, 2 * NCT * 1024 + j * NC:2 * NCT * 1024 + (j + 1) * NC]

            def w_pref_a(oi, half):
                bw = bW0 if half == 0 else bW1
                ED("gpsimd", lambda e, oi=oi, half=half: e.dma_start(
                    out=wv_ah[half][:, 0, :, :],
                    in_=w1T[:, oi * D_MODEL:(oi + 1) * D_MODEL]),
                   writes=[bw])
                ED("gpsimd", lambda e, oi=oi, half=half: e.dma_start(
                    out=wv_ah[half][:, 1, :, :],
                    in_=w2T[:, oi * D_MODEL:(oi + 1) * D_MODEL]),
                   writes=[bw])

            def w_pref_h(srcT, oi, half):
                bw = bW0 if half == 0 else bW1
                ED("gpsimd", lambda e, oi=oi, half=half, srcT=srcT: e.dma_start(
                    out=wv_hh[half][:, :, :],
                    in_=srcT[:, oi * D_HID:(oi + 1) * D_HID]),
                   writes=[bw])

            for n in range(NCH):
                load_block(src_dram, src_buf, 0, n)
            w_pref_a(0, 0)
            for n in range(NCH):
                ln_chunk(n, False, False, gstage, [bGD])

            for blk in range(NB):
                b0 = blk * Tb
                if blk > 0:
                    w_pref_a(0, 0)
                for oi in range(NHT):
                    half = oi % 2
                    bw = bW0 if half == 0 else bW1
                    if oi + 1 < NHT:
                        w_pref_a(oi + 1, 1 - half)
                    for n in range(NCH):
                        c0 = n * NC
                        k1 = 2 * n
                        k2 = k1 + 1
                        G([(lambda e, ct=ct, c0=c0, half=half, k1=k1: e.matmul(
                            out=pb[k1][:], lhsT=wv_ah[half][:, 0, ct, :],
                            rhs=t_xn[:, ct * Tb + c0:ct * Tb + c0 + NC],
                            start=(ct == 0), stop=(ct == NCT - 1))) for ct in range(NCT)],
                          reads=[bw], writes=[bPB[k1]],
                          fn_reads=[[bXN[n][ct]] for ct in range(NCT)])
                        G([(lambda e, ct=ct, c0=c0, half=half, k2=k2: e.matmul(
                            out=pb[k2][:], lhsT=wv_ah[half][:, 1, ct, :],
                            rhs=t_xn[:, ct * Tb + c0:ct * Tb + c0 + NC],
                            start=(ct == 0), stop=(ct == NCT - 1))) for ct in range(NCT)],
                          reads=[bw], writes=[bPB[k2]],
                          fn_reads=[[bXN[n][ct]] for ct in range(NCT)])
                        E("scalar", lambda e, oi=oi, k1=k1: e.activation(
                            out=t_tmpA[:], in_=pb[k1][:], func=AF.Silu, scale=s1c(oi)),
                          "a", reads=[bPB[k1], bCONST], writes=[bTA])
                        E("vector", lambda e, oi=oi, c0=c0, k2=k2: e.scalar_tensor_tensor(
                            out=t_h[:, oi * Tb + c0:oi * Tb + c0 + NC], in0=pb[k2][:],
                            scalar=s2c(oi), in1=t_tmpA[:], op0=ALU.mult, op1=ALU.mult),
                          "v", reads=[bPB[k2], bTA, bCONST], writes=[bH])

                for phase in ("h1", "h2"):
                    srcT = h1T if phase == "h1" else h2T
                    src_arena = t_h if phase == "h1" else t_g
                    dst_arena = t_g if phase == "h1" else t_h
                    src_bufs = [bH] if phase == "h1" else [bG, bGD]
                    dst_bufs = [bG, bGD] if phase == "h1" else [bH]
                    scol = sh1c if phase == "h1" else sh2c
                    w_pref_h(srcT, 0, 0)
                    for oi in range(NHT):
                        half = oi % 2
                        bw = bW0 if half == 0 else bW1
                        if oi + 1 < NHT:
                            w_pref_h(srcT, oi + 1, 1 - half)
                        bks = [(oi * NCH + n) % 4 for n in range(NCH)]
                        fns = [(lambda e, hi=hi, n=n, half=half, bk=bks[n], src_arena=src_arena: e.matmul(
                            out=pb[bk][:], lhsT=wv_hh[half][:, hi, :],
                            rhs=src_arena[:, hi * Tb + n * NC:hi * Tb + n * NC + NC],
                            start=(hi == 0), stop=(hi == NHT - 1)))
                            for hi in range(NHT) for n in range(NCH)]
                        G(fns, reads=[bw] + src_bufs, writes=[bPB[b] for b in bks])
                        for n in range(NCH):
                            c0 = n * NC
                            E("scalar", lambda e, oi=oi, c0=c0, bk=bks[n], dst_arena=dst_arena, scol=scol: e.activation(
                                out=dst_arena[:, oi * Tb + c0:oi * Tb + c0 + NC], in_=pb[bk][:],
                                func=AF.Gelu, scale=scol(oi)),
                              "a", reads=[bPB[bks[n]], bCONST], writes=dst_bufs)

                w_pref_h(w3T, 0, 0)
                pref = 0
                for n in range(NCH):
                    c0 = n * NC
                    for oi in range(NCT):
                        half = pref % 2
                        bw = bW0 if half == 0 else bW1
                        if pref + 1 < NCT * NCH:
                            w_pref_h(w3T, (pref + 1) % NCT, 1 - half)
                        pref += 1
                        if blk + 1 < NB and NCH > 1 and n == NCH - 1 and oi == 3:
                            ln_math(0, False, False, gstage, [bGD])
                        bk = (oi * NCH + n) % 4
                        G([(lambda e, hi=hi, c0=c0, half=half, bk=bk: e.matmul(
                            out=pb[bk][:], lhsT=wv_hh[half][:, hi, :],
                            rhs=t_h[:, hi * Tb + c0:hi * Tb + c0 + NC],
                            start=(hi == 0), stop=(hi == NHT - 1))) for hi in range(NHT)],
                          reads=[bw, bH], writes=[bPB[bk]])
                        E("vector", lambda e, oi=oi, c0=c0, bk=bk: e.scalar_tensor_tensor(
                            out=gf32[:, oi * Tb + c0:oi * Tb + c0 + NC], in0=pb[bk][:],
                            scalar=s3c(oi), in1=t_xcm[:, oi * Tb + c0:oi * Tb + c0 + NC],
                            op0=ALU.mult, op1=ALU.add),
                          "v", reads=[bPB[bk], bXCM[n], bCONST], writes=[bG])
                    ED("sync", lambda e, b0=b0, n=n: e.dma_start(
                        out=dview(dst_dram)[:, :, b0 + n * NC:b0 + n * NC + NC],
                        in_=gf32[:, 0:NCT * Tb].rearrange("p (a t) -> p a t", t=Tb)[:, :, n * NC:(n + 1) * NC]),
                       reads=[bG], writes=[dst_buf])
                    if blk + 1 < NB:
                        load_block(src_dram, src_buf, b0 + Tb, n)
                        ln_stage(n, gstage, [bGD])
                if blk + 1 < NB:
                    if NCH > 1:
                        ln_math(1, False, False, gstage, [bGD])
                    else:
                        ln_math(0, False, False, gstage, [bGD])

        awv = t_g[:, 0:2 * NCT * 1024].rearrange("p (w a n) -> p w a n", w=2, a=NCT)
        DRES_F32 = NCT * 1024
        dres = lambda dt: gf32[:, DRES_F32 + dt * Tb:DRES_F32 + (dt + 1) * Tb]
        wstage = lambda j: t_wst[:, j * NC:(j + 1) * NC]

        def build_dres():
            for dt in range(NCT):
                E("vector", lambda e, dt=dt: e.tensor_copy(out=dres(dt)[:, 0:1], in_=pwc(C_TDQ, dt)),
                  "v", reads=[bPOW], writes=[bGD])
                for k in range(LGT):
                    E("vector", lambda e, dt=dt, k=k: e.tensor_scalar_mul(
                        out=dres(dt)[:, 1 << k:2 << k], in0=dres(dt)[:, 0:1 << k],
                        scalar1=pwc(POW0 + 8 * k, dt)), "v", reads=[bPOW, bGD], writes=[bGD])

        def attn():
            ED("gpsimd", lambda e: e.dma_start(out=awv[:, 0, :, :], in_=wkT), writes=[bG])
            ED("gpsimd", lambda e: e.dma_start(out=awv[:, 1, :, :], in_=wvT), writes=[bG])
            build_dres()
            E("vector", lambda e: e.memset(pw8(C_SIG), 0.0), "v", writes=[bPOW])

            def store_xn(b0):
                for n in range(NCH):
                    ED("sync", lambda e, b0=b0, n=n: e.dma_start(
                        out=dview(xnd)[:, :, b0 + n * NC:b0 + n * NC + NC],
                        in_=t_xn[:].rearrange("p (a t) -> p a t", t=Tb)[:, :, n * NC:(n + 1) * NC]),
                       reads=[bXN[n][ct] for ct in range(NCT)], writes=[bXND])

            for n in range(NCH):
                load_block(x1d, bX1D, 0, n)
            for n in range(NCH):
                ln_chunk(n, True, True, wstage, [bW0, bW1])
            store_xn(0)
            for blk in range(NB):
                b0 = blk * Tb
                for dt in range(NCT):
                    if blk + 1 < NB and dt == 1:
                        for n in range(NCH):
                            load_block(x1d, bX1D, b0 + Tb, n)
                        ln_stage(0, wstage, [bW0, bW1])
                    for n in range(NCH):
                        c0 = n * NC
                        idx = (dt * NCH + n) % 2
                        kk, kv_ = 2 * idx, 2 * idx + 1
                        G([(lambda e, ct=ct, dt=dt, c0=c0, kk=kk: e.matmul(
                            out=pb[kk][:], lhsT=awv[:, 0, ct, dt * 128:(dt + 1) * 128],
                            rhs=t_xn[:, ct * Tb + c0:ct * Tb + c0 + NC],
                            start=(ct == 0), stop=(ct == NCT - 1))) for ct in range(NCT)],
                          reads=[bG], writes=[bPB[kk]],
                          fn_reads=[[bXN[n][ct]] for ct in range(NCT)])
                        G([(lambda e, ct=ct, dt=dt, c0=c0, kv_=kv_: e.matmul(
                            out=pb[kv_][:], lhsT=awv[:, 1, ct, dt * 128:(dt + 1) * 128],
                            rhs=t_xn[:, ct * Tb + c0:ct * Tb + c0 + NC],
                            start=(ct == 0), stop=(ct == NCT - 1))) for ct in range(NCT)],
                          reads=[bG], writes=[bPB[kv_]],
                          fn_reads=[[bXN[n][ct]] for ct in range(NCT)])
                        E("scalar", lambda e, kk=kk: e.copy(out=t_tmpA[:], in_=pb[kk][:]),
                          "a", reads=[bPB[kk]], writes=[bTA])
                        E("vector", lambda e, dt=dt, b0=b0, c0=c0, kv_=kv_: e.tensor_mul(
                            out=t_h[:, dt * S + b0 + c0:dt * S + b0 + c0 + NC],
                            in0=t_tmpA[:], in1=pb[kv_][:]),
                          "v", reads=[bTA, bPB[kv_]], writes=[bH])
                        E("vector", lambda e, dt=dt, b0=b0, c0=c0: e.scalar_tensor_tensor(
                            out=t_tmpB[:], in0=t_h[:, dt * S + b0 + c0:dt * S + b0 + c0 + NC],
                            scalar=1.0, in1=dres(dt)[:, c0:c0 + NC], op0=ALU.mult, op1=ALU.mult,
                            accum_out=t_tmpC[:, 0:1]), "v", reads=[bH, bGD], writes=[bTB, bTC])
                        E("vector", lambda e, dt=dt: e.tensor_add(
                            out=pwc(C_SIG, dt), in0=pwc(C_SIG, dt), in1=t_tmpC[:, 0:1]),
                          "v", reads=[bTC], writes=[bPOW])
                if blk + 1 < NB:
                    ln_math(0, True, True, wstage, [bW0, bW1])
                    for n in range(1, NCH):
                        ln_stage(n, wstage, [bW0, bW1])
                        ln_math(n, True, True, wstage, [bW0, bW1])
                    store_xn(b0 + Tb)
                for dt in range(NCT):
                    E("vector", lambda e, dt=dt: e.tensor_scalar_mul(
                        out=dres(dt)[:], in0=dres(dt)[:], scalar1=pwc(POW0 + 8 * LGT, dt)),
                      "v", reads=[bPOW], writes=[bGD])

            ED("gpsimd", lambda e: e.dma_start(out=ccin.ap(), in_=pw8(C_SIG)),
               reads=[bPOW], writes=[bCCIN])
            ED("gpsimd", lambda e: e.dma_start(out=awv[:, 0, :, :], in_=wrT), writes=[bG])
            ED("gpsimd", lambda e: e.dma_start(out=awv[:, 1, :, :], in_=woT), writes=[bG])
            E("gpsimd", lambda e: e.collective_compute(
                "AllGather", ALU.bypass, replica_groups=[list(range(8))],
                ins=[ccin.ap().opt()], outs=[ccout.ap().opt()]), "cc", 1,
              reads=[bCCIN], writes=[bCCOUT])
            ED("gpsimd", lambda e: e.dma_start(
                out=t_sg[:].rearrange("p (r c) -> p r c", c=8),
                in_=ccout.ap().rearrange("(r p) c -> p r c", p=128)),
               reads=[bCCOUT], writes=[bSG])

            def load_xn(b0):
                for n in range(NCH):
                    ED("sync", lambda e, b0=b0, n=n: e.dma_start(
                        out=t_xn[:].rearrange("p (a t) -> p a t", t=Tb)[:, :, n * NC:(n + 1) * NC],
                        in_=dview(xnd)[:, :, b0 + n * NC:b0 + n * NC + NC]),
                       reads=[bXND], writes=[bXN[n][ct] for ct in range(NCT)])

            build_dres()
            for n in range(NCH):
                load_block(x1d, bX1D, 0, n)
            load_xn(0)
            E("vector", lambda e: e.memset(pw8(C_CAR), 0.0), "v", writes=[bPOW])
            sgv = t_sg[:].rearrange("p (r c) -> p r c", c=8)
            for r in range(8):
                E("vector", lambda e, r=r: e.scalar_tensor_tensor(
                    out=pw8(C_CAR), in0=sgv[:, r, :], scalar=t_ccs[:, 4 + r:5 + r], in1=pw8(C_CAR),
                    op0=ALU.mult, op1=ALU.add), "v", reads=[bSG, bCC, bPOW], writes=[bPOW])

            for blk in range(NB):
                b0 = blk * Tb
                for dt in range(NCT):
                    for n in range(NCH):
                        c0 = n * NC
                        bk = (dt * NCH + n) % 4
                        G([(lambda e, ct=ct, dt=dt, c0=c0, bk=bk: e.matmul(
                            out=pb[bk][:], lhsT=awv[:, 0, ct, dt * 128:(dt + 1) * 128],
                            rhs=t_xn[:, ct * Tb + c0:ct * Tb + c0 + NC],
                            start=(ct == 0), stop=(ct == NCT - 1))) for ct in range(NCT)],
                          reads=[bG], writes=[bPB[bk]],
                          fn_reads=[[bXN[n][ct]] for ct in range(NCT)])
                        E("scalar", lambda e, bk=bk: e.activation(
                            out=t_tmpA[:], in_=pb[bk][:], func=AF.Sigmoid),
                          "a", reads=[bPB[bk]], writes=[bTA])
                        E("vector", lambda e, dt=dt, b0=b0, c0=c0: e.tensor_mul(
                            out=t_tmpB[:], in0=t_h[:, dt * S + b0 + c0:dt * S + b0 + c0 + NC],
                            in1=dres(dt)[:, c0:c0 + NC]), "v", reads=[bH, bGD], writes=[bTB])
                        E("vector", lambda e, dt=dt: e.tensor_copy(out=t_scan[:, 0:1], in_=pwc(C_CAR, dt)),
                          "v", reads=[bPOW], writes=[bSCAN])
                        E("vector", lambda e, dt=dt: e.tensor_tensor_scan(
                            out=t_scan[:, 1:NC + 1], data0=t_ones[:], data1=t_tmpB[:],
                            initial=pwc(C_CAR, dt), op0=ALU.mult, op1=ALU.add),
                          "v", reads=[bTB, bPOW, bCONST], writes=[bSCAN])
                        E("vector", lambda e, dt=dt: e.tensor_copy(out=pwc(C_CAR, dt), in_=t_scan[:, NC:NC + 1]),
                          "v", reads=[bSCAN], writes=[bPOW])
                        E("scalar", lambda e, dt=dt, b0=b0, c0=c0: e.mul(
                            out=t_tmpC[:], in_=t_h[:, dt * S + b0 + c0:dt * S + b0 + c0 + NC],
                            mul=pwc(C_TF, dt)), "a", reads=[bH, bPOW], writes=[bTC])
                        E("vector", lambda e, dt=dt: e.scalar_tensor_tensor(
                            out=t_tmpB[:], in0=t_scan[:, 0:NC], scalar=pwc(C_TD, dt), in1=t_tmpC[:],
                            op0=ALU.mult, op1=ALU.add), "v", reads=[bSCAN, bTC, bPOW], writes=[bTB])
                        E("gpsimd", lambda e, dt=dt, c0=c0: e.tensor_mul(
                            out=t_wst[:, dt * Tb + c0:dt * Tb + c0 + NC], in0=t_tmpB[:], in1=t_tmpA[:]),
                          "p", reads=[bTB, bTA], writes=[bW0 if dt < NCT // 2 else bW1])
                for ct in range(NCT):
                    for n in range(NCH):
                        c0 = n * NC
                        bk = (ct * NCH + n) % 4
                        G([(lambda e, dt=dt, ct=ct, c0=c0, bk=bk: e.matmul(
                            out=pb[bk][:], lhsT=awv[:, 1, dt, ct * 128:(ct + 1) * 128],
                            rhs=t_wst[:, dt * Tb + c0:dt * Tb + c0 + NC],
                            start=(dt == 0), stop=(dt == NCT - 1))) for dt in range(NCT)],
                          reads=[bG, bW0, bW1], writes=[bPB[bk]])
                        E("vector", lambda e, ct=ct, c0=c0, bk=bk: e.tensor_add(
                            out=t_xcm[:, ct * Tb + c0:ct * Tb + c0 + NC], in0=pb[bk][:],
                            in1=t_xcm[:, ct * Tb + c0:ct * Tb + c0 + NC]),
                          "v", reads=[bPB[bk]], writes=[bXCM[n]])
                for n in range(NCH):
                    store_block(x2d, bX2D, b0, n)
                    if blk + 1 < NB:
                        load_block(x1d, bX1D, b0 + Tb, n)
                if blk + 1 < NB:
                    load_xn(b0 + Tb)
                for dt in range(NCT):
                    E("vector", lambda e, dt=dt: e.tensor_scalar_mul(
                        out=dres(dt)[:], in0=dres(dt)[:], scalar1=pwc(POW0 + 8 * LGT, dt)),
                      "v", reads=[bPOW], writes=[bGD])

        ffn(xT, Buf("xTsrc"), x1d, bX1D)
        attn()
        ffn(x2d, bX2D, outT, Buf("outDst"))

        final_waits = [P.dma_last[s] for s in P.DMA_ROT["sync"] if s in P.dma_last]
        final_waits += [P.dma_last[s] for s in P.DMA_ROT["gpsimd"] if s in P.dma_last]
        P.ops["sync"].append((P._dedupe("sync", final_waits), None, None, 0))

        block = es.enter_context(nc.Block())

        def replay(eng_obj, name):
            for wl, fn, sem, inc in P.ops[name]:
                for (s, v) in wl:
                    eng_obj.wait_ge(sems[s], v)
                if fn is not None:
                    ins = fn(eng_obj)
                    if sem is not None:
                        ins.then_inc(sems[sem], inc)

        @block.sync
        def _(e):
            replay(e, "sync")

        @block.gpsimd
        def _(e):
            replay(e, "gpsimd")

        @block.tensor
        def _(e):
            replay(e, "tensor")

        @block.scalar
        def _(e):
            replay(e, "scalar")

        @block.vector
        def _(e):
            replay(e, "vector")

    return nc


def _prep_inputs(inputs, S):
    x = np.asarray(inputs["L0_attn_x"])
    B, T, C = x.shape
    q_per_b = T // S
    n_cores = B * q_per_b
    col = lambda v: np.ascontiguousarray(np.asarray(v, dtype=np.float32).reshape(-1, 128).T)

    def tile4(a, no, nj, na):
        a = np.asarray(a, dtype=np.float32).reshape(no, nj, na, 128)
        return np.ascontiguousarray(a.transpose(3, 0, 2, 1).reshape(128, no * na * nj))

    shared = dict(
        w1T=tile4(inputs["L0_ffn_w1_w"], 32, 128, 8),
        w2T=tile4(inputs["L0_ffn_w2_w"], 32, 128, 8),
        h1T=tile4(inputs["L0_ffn_h1_w"], 32, 128, 32),
        h2T=tile4(inputs["L0_ffn_h2_w"], 32, 128, 32),
        w3T=tile4(inputs["L0_ffn_w3_w"], 8, 128, 32),
        wkT=tile4(inputs["L0_attn_Wk"], 1, 1024, 8),
        wvT=tile4(inputs["L0_attn_Wv"], 1, 1024, 8),
        wrT=tile4(inputs["L0_attn_Wr"], 1, 1024, 8),
        woT=tile4(inputs["L0_attn_Wo"], 1, 1024, 8),
    )
    pcarr = np.concatenate([
        col(inputs["L0_ffn_w1_s"]), col(inputs["L0_ffn_w2_s"]),
        col(inputs["L0_ffn_h1_s"]), col(inputs["L0_ffn_h2_s"]),
        col(inputs["L0_ffn_w3_s"]),
        col(inputs["L0_attn_time_decay"]), col(inputs["L0_attn_time_first"]),
        col(inputs["L0_attn_ln_gamma"]), col(inputs["L0_attn_ln_beta"]),
    ], axis=1).astype(np.float32)
    in_maps = []
    for c in range(n_cores):
        b, q = divmod(c, q_per_b)
        xs = np.ascontiguousarray(x[b, q * S:(q + 1) * S, :].astype(np.float32).T)
        qb1, qb2 = float(q & 1), float((q >> 1) & 1)
        ccarr = np.zeros((128, 12), np.float32)
        ccarr[:, 0] = qb1
        ccarr[:, 1] = 1.0 - qb1
        ccarr[:, 2] = qb2
        ccarr[:, 3] = 1.0 - qb2
        for r in range(n_cores):
            rb, rq = divmod(r, q_per_b)
            if rb == b and rq < q:
                ccarr[:, 4 + r] = 1.0
        m = dict(shared)
        m["xT"] = xs
        m["pc"] = pcarr
        m["cci"] = ccarr
        in_maps.append(m)
    return in_maps, n_cores


def kernel(**inputs):
    x = np.asarray(inputs["L0_attn_x"])
    B, T, C = x.shape
    S = (B * T) // 8
    Tb = min(1024, S)
    in_maps, n_cores = _prep_inputs(inputs, S)
    nc = build_program(S, Tb)
    res = run_bass_kernel_spmd(nc, in_maps, core_ids=list(range(n_cores)))
    out = np.empty((B, T, C), np.float32)
    q_per_b = T // S
    for c in range(n_cores):
        b, q = divmod(c, q_per_b)
        out[b, q * S:(q + 1) * S, :] = np.asarray(res.results[c]["outT"]).T
    return out
